# version 04 (final)
# baseline (speedup 1.0000x reference)
import numpy as np
from contextlib import ExitStack
import concourse.bass as bass
import concourse.mybir as mybir
from concourse.bass_utils import run_bass_kernel_spmd

F32 = mybir.dt.float32
BF16 = mybir.dt.bfloat16
ALU = mybir.AluOpType
ACT = mybir.ActivationFunctionType
AXX = mybir.AxisListType.X

B, L, DM, DI = 2, 8192, 1024, 2048
DQ, S, RK, CV = 512, 16, 64, 4
TQ = 512
NT = L // TQ
NDB = DQ // 128
TOK = L // 4
NTT = TOK // TQ
E, HF = 8, 2048
EPS = 1e-5
PIPELINE = True
import os
K_NT = int(os.environ.get("K_NT", "16"))
K_DIRS = int(os.environ.get("K_DIRS", "2"))
K_E = int(os.environ.get("K_E", "8"))
K_FFN = int(os.environ.get("K_FFN", "1"))
K_CC = int(os.environ.get("K_CC", "1"))
K_FP = int(os.environ.get("K_FP", "3"))
K_LN = int(os.environ.get("K_LN", "1"))
K_DBG = os.environ.get("K_DBG", "")
K_RT = int(os.environ.get("K_RT", "1"))


def _bap(ap_2d, count):
    a = list(ap_2d.ap)
    assert a[0][1] == 1
    return bass.AP(ap_2d.tensor, ap_2d.offset, [[0, count]] + a[1:])


def _mid_bcast(ap_2d, count):
    a = list(ap_2d.ap)
    return bass.AP(ap_2d.tensor, ap_2d.offset, [a[0], [0, count], a[1]])


class _EarlyEmit(Exception):
    pass


class P:

    def __init__(self, nc):
        self.nc = nc
        self.streams = {k: [] for k in ("sync", "scalar", "vector", "tensor", "gpsimd")}
        self.cnt = {}
        self.sems = {}

    def sem(self, ctx, name):
        self.sems[name] = ctx.enter_context(self.nc.semaphore(name))
        self.cnt[name] = 0

    def op(self, eng, fn, waits=(), incs=()):
        waits = [(s, v) for (s, v) in waits if v > 0]
        for s, a in incs:
            self.cnt[s] += a
        after = {s: self.cnt[s] for (s, _a) in incs}
        sems = self.sems

        def run(engine, fn=fn, waits=tuple(waits), incs=tuple(incs)):
            for s, v in waits:
                engine.wait_ge(sems[s], v)
            ins = fn(engine)
            for s, a in incs:
                ins.then_inc(sems[s], a)

        self.streams[eng].append(run)
        return after

    def wait(self, eng, waits):
        waits = [(s, v) for (s, v) in waits if v > 0]
        if not waits:
            return
        sems = self.sems

        def run(engine, waits=tuple(waits)):
            for s, v in waits:
                engine.wait_ge(sems[s], v)

        self.streams[eng].append(run)

    def now(self, sem):
        return self.cnt[sem]


def build_program():
    nc = bass.Bass("TRN2", target_bir_lowering=False, debug=False)
    dr = lambda n, sh, dt, k=None: (
        nc.dram_tensor(n, sh, dt, kind=k) if k else nc.dram_tensor(n, sh, dt)
    )
    x = dr("x", [DM, L], F32, "ExternalInput")
    xtok = dr("xtok", [DM, TOK], F32, "ExternalInput")
    winT = dr("winT", [2, DM, 1024], F32, "ExternalInput")
    convw = dr("convw", [2, 128, NDB * CV], F32, "ExternalInput")
    convb = dr("convb", [2, 128, NDB], F32, "ExternalInput")
    dtbi = dr("dtbi", [2, 128, NDB], F32, "ExternalInput")
    alog = dr("alog", [2, 128, NDB * S], F32, "ExternalInput")
    dcol = dr("dcol", [2, 128, NDB], F32, "ExternalInput")
    wxT = dr("wxT", [2, DQ, 96], F32, "ExternalInput")
    wdtT = dr("wdtT", [2, RK, DQ], F32, "ExternalInput")
    woutT = dr("woutT", [2, DQ, DM], F32, "ExternalInput")
    upT = dr("upT", [E, DM, HF], F32, "ExternalInput")
    gateT = dr("gateT", [E, DM, HF], F32, "ExternalInput")
    downT = dr("downT", [E, HF, DM], F32, "ExternalInput")
    wupsT = dr("wupsT", [DM, HF], F32, "ExternalInput")
    wgsT = dr("wgsT", [DM, HF], F32, "ExternalInput")
    wdsT = dr("wdsT", [HF, DM], F32, "ExternalInput")
    routT = dr("routT", [DM, E], F32, "ExternalInput")
    ident = dr("ident", [128, 128], F32, "ExternalInput")
    ones1 = dr("ones1", [128, 1], F32, "ExternalInput")
    onesr = dr("onesr", [1, 128], F32, "ExternalInput")
    out = dr("out", [DM, TOK], F32, "ExternalOutput")
    ccin = [dr(f"ccin{i}", [96, TQ], F32) for i in range(2)]
    ccout = [dr(f"ccout{i}", [96, TQ], F32) for i in range(2)]
    accd = [dr(f"acc{d}", [4, DM, TOK], BF16) for d in range(2)]
    rsd = [dr(f"rs{d}", [DM, TOK], BF16) for d in range(2)]
    mskd = dr("mskd", [E, 4 * 512], BF16)
    mxd = dr("mxd", [1, 4 * 512], F32)
    GROUPS = [[0, 1, 2, 3], [4, 5, 6, 7]]

    with ExitStack() as ctx:
        p = P(nc)
        for s in ("g", "hw", "t", "v", "s", "cc", "ph"):
            p.sem(ctx, s)
        sb = lambda n, sh, dt: ctx.enter_context(nc.sbuf_tensor(n, sh, dt))
        ps = lambda n, sh: ctx.enter_context(nc.psum_tensor(n, sh, F32))

        pA = [ps(f"pA{i}", [128, TQ]) for i in range(3)]
        pX = ps("pX", [128, TQ])
        pD = [ps(f"pD{i}", [128, TQ]) for i in range(2)]
        pO = [ps(f"pO{i}", [128, TQ]) for i in range(2)]

        block = ctx.enter_context(nc.Block())

        with ExitStack() as mctx:
            msb = lambda n, sh, dt: mctx.enter_context(nc.sbuf_tensor(n, sh, dt))
            w_in = msb("w_in", [128, 8 * 1024], BF16)
            wx = msb("wx", [128, NDB * 96], BF16)
            wdt = [msb(f"wdt{d}", [RK, DQ], BF16) for d in range(2)]
            wout = [msb(f"wout{d}", [128, NDB * DM], BF16) for d in range(2)]
            cvw = msb("cvw", [128, NDB * CV], F32)
            cvb = msb("cvb", [128, NDB], F32)
            dtb = [msb(f"dtb{d}", [128, NDB], F32) for d in range(2)]
            asb = [msb(f"asb{d}", [128, NDB * S], F32) for d in range(2)]
            dsb = [msb(f"dsb{d}", [128, NDB], F32) for d in range(2)]
            alg = msb("alg", [128, NDB * S], F32)
            sx = [msb(f"sx{i}", [128, 8 * TQ], BF16) for i in range(2)]
            xib = [msb(f"xib{i}", [128, NDB * (TQ + 3)], BF16) for i in range(2)]
            cac = msb("cac", [128, TQ], BF16)
            xcb = [msb(f"xcb{i}", [128, NDB * TQ], BF16) for i in range(2)]
            szb = [msb(f"szb{i}", [128, NDB * TQ], BF16) for i in range(2)]
            xdbp = [msb(f"xdbp{i}", [96, TQ], F32) for i in range(2)]
            dtf = [msb(f"dtf{i}", [RK, TQ], BF16) for i in range(2)]
            bca = [msb(f"bca{i}", [128, S * TQ], BF16) for i in range(2)]
            cca = [msb(f"cca{i}", [128, S * TQ], BF16) for i in range(2)]
            dlt = msb("dlt", [128, NDB * TQ], BF16)
            dub = msb("dub", [128, NDB * TQ], BF16)
            sp = [msb(f"sp{i}", [128, TQ], BF16) for i in range(3)]
            dAp = [msb(f"dAp{i}", [128, TQ], BF16) for i in range(4)]
            wbf = msb("wbf", [128, NDB * TQ], BF16)
            hbf = msb("hbf", [128, NDB * TQ], BF16)
            pbf = msb("pbf", [128, NDB * TQ], BF16)
            yac = msb("yac", [128, NDB * TQ], BF16)
            cry = msb("cry", [128, NDB * S], F32)
            y2b = [msb(f"y2b{i}", [128, NDB * TQ], BF16) for i in range(2)]
            otp = [msb(f"otp{i}", [128, 8 * TQ], BF16) for i in range(2)]

            bank_free = {id(b): [] for b in pA + [pX] + pD + pO}
            bank_ready = {}
            dA_free = [[] for _ in range(4)]

            g0 = p.op("gpsimd", lambda g: g.dma_start(
                out=cvw[:], in_=convw.ap()[0]), incs=[("g", 16)])
            for d in range(2):
                p.op("gpsimd", lambda g, d=d: g.dma_start(
                    out=dtb[d][:], in_=dtbi.ap()[d]), incs=[("g", 16)])
                p.op("gpsimd", lambda g, d=d: g.dma_start(
                    out=dsb[d][:], in_=dcol.ap()[d]), incs=[("g", 16)])
                p.op("gpsimd", lambda g, d=d: g.dma_start(
                    out=alg[:], in_=alog.ap()[d]), incs=[("g", 16)])
                gA = p.now("g")
                sA = p.op("scalar", lambda s, d=d: s.activation(
                    asb[d][:], alg[:], ACT.Exp), waits=[("g", gA)], incs=[("s", 1)])
                p.op("vector", lambda v, d=d: v.tensor_scalar_mul(
                    asb[d][:], asb[d][:], -1.0), waits=[("s", sA["s"])], incs=[("v", 1)])
                p.wait("gpsimd", [("s", sA["s"])])

            wconsts = {"cvb": None}
            p.op("gpsimd", lambda g: g.dma_start(out=cvb[:], in_=convb.ap()[0]),
                 incs=[("g", 16)])

            st_hw_otp = [{}, {}]
            sx_free = [[], []]
            xib_free = [[], []]
            xcb_free = [[], []]
            szb_free = [[], []]
            y2_free = [[], []]
            bca_free = [[], []]
            dtf_free = [[], []]
            xdbp_free = [[], []]
            ccin_free = [[], []]

            state = {}

            def load_dir_weights(d):
                waits = state.get("w_in_free", [])
                p.op("gpsimd", lambda g: g.dma_start(
                    out=w_in[:].rearrange("p (k c) -> p k c", k=8),
                    in_=winT.ap()[d].rearrange("(k p) c -> p k c", p=128)),
                    waits=waits, incs=[("g", 16)])
                p.op("gpsimd", lambda g: g.dma_start(
                    out=wx[:].rearrange("p (k c) -> p k c", k=NDB),
                    in_=wxT.ap()[d].rearrange("(k p) c -> p k c", p=128)),
                    waits=waits, incs=[("g", 16)])
                p.op("gpsimd", lambda g: g.dma_start(
                    out=wdt[d][:], in_=wdtT.ap()[d]), incs=[("g", 16)])
                p.op("gpsimd", lambda g: g.dma_start(
                    out=wout[d][:].rearrange("p (k c) -> p k c", k=NDB),
                    in_=woutT.ap()[d].rearrange("(k p) c -> p k c", p=128)),
                    incs=[("g", 16)])
                if d == 1:
                    p.op("gpsimd", lambda g: g.dma_start(
                        out=cvw[:], in_=convw.ap()[1]), waits=state.get("cvw_free", []),
                        incs=[("g", 16)])
                    p.op("gpsimd", lambda g: g.dma_start(
                        out=cvb[:], in_=convb.ap()[1]), waits=state.get("cvb_free", []),
                        incs=[("g", 16)])
                return p.now("g")

            def stage_a(d, j):
                pj = j % 2
                t0 = (j if d == 0 else NT - 1 - j) * TQ
                gx = p.op("gpsimd", lambda g: g.dma_start(
                    out=sx[pj][:].rearrange("p (k t) -> p k t", k=8),
                    in_=x.ap()[:, t0:t0 + TQ].rearrange("(k p) t -> p k t", p=128)),
                    waits=sx_free[pj] + [("g", state["wload"])], incs=[("g", 16)])["g"]
                evac_vals = []
                for cb in range(8):
                    bank = pA[cb % 3]
                    bw = bank_free[id(bank)]
                    for k in range(8):
                        p.op("tensor", lambda t, cb=cb, k=k, bank=bank, pj=pj: t.matmul(
                            bank[:], w_in[:, k * 1024 + cb * 128:k * 1024 + (cb + 1) * 128],
                            sx[pj][:, k * TQ:(k + 1) * TQ],
                            start=(k == 0), stop=(k == 7)),
                            waits=([("g", gx)] + bw) if k == 0 else [],
                            incs=[("t", 1)] if k == 7 else [])
                    tv = p.now("t")
                    if cb < 4:
                        off = 3 if d == 0 else 0
                        ev = p.op("scalar", lambda s, cb=cb, pj=pj, bank=bank, off=off: s.copy(
                            xib[pj][:, cb * (TQ + 3) + off: cb * (TQ + 3) + off + TQ], bank[:]),
                            waits=[("t", tv)] + xib_free[pj], incs=[("s", 1)])["s"]
                    else:
                        ev = p.op("scalar", lambda s, cb=cb, pj=pj, bank=bank: s.activation(
                            szb[pj][:, (cb - 4) * TQ:(cb - 3) * TQ], bank[:], ACT.Silu),
                            waits=[("t", tv)] + szb_free[pj], incs=[("s", 1)])["s"]
                    bank_free[id(bank)] = [("s", ev)]
                    evac_vals.append(ev)
                xib_free[pj] = []
                szb_free[pj] = []
                if j == 0:
                    for db in range(NDB):
                        hoff = 0 if d == 0 else TQ
                        p.op("vector", lambda v, db=db, pj=pj, hoff=hoff: v.memset(
                            xib[pj][:, db * (TQ + 3) + hoff: db * (TQ + 3) + hoff + 3], 0.0),
                            waits=[("s", evac_vals[db])], incs=[("v", 1)])
                else:
                    for db in range(NDB):
                        if d == 0:
                            sl_d = (pj, db * (TQ + 3), db * (TQ + 3) + 3)
                            sl_s = (1 - pj, db * (TQ + 3) + TQ, db * (TQ + 3) + TQ + 3)
                        else:
                            sl_d = (pj, db * (TQ + 3) + TQ, db * (TQ + 3) + TQ + 3)
                            sl_s = (1 - pj, db * (TQ + 3) + 3, db * (TQ + 3) + 6)
                        p.op("vector", lambda v, sl_d=sl_d, sl_s=sl_s: v.tensor_copy(
                            xib[sl_d[0]][:, sl_d[1]:sl_d[2]], xib[sl_s[0]][:, sl_s[1]:sl_s[2]]),
                            waits=[("s", evac_vals[db])], incs=[("v", 1)])
                xc_vals = []
                for db in range(NDB):
                    base = db * (TQ + 3)
                    p.op("vector", lambda v, db=db, pj=pj, base=base: v.tensor_scalar(
                        cac[:], xib[pj][:, base:base + TQ], cvw[:, db * CV:db * CV + 1],
                        None, ALU.mult),
                        waits=[("s", evac_vals[db]), ("g", state["wload"])]
                        + state.get("cac_free", []),
                        incs=[("v", 1)])
                    for k in range(1, CV):
                        p.op("vector", lambda v, db=db, pj=pj, base=base, k=k: v.scalar_tensor_tensor(
                            cac[:], xib[pj][:, base + k:base + k + TQ],
                            cvw[:, db * CV + k:db * CV + k + 1], cac[:], ALU.mult, ALU.add),
                            incs=[("v", 1)])
                    vv = p.now("v")
                    xcv = p.op("scalar", lambda s, db=db, pj=pj: s.activation(
                        xcb[pj][:, db * TQ:(db + 1) * TQ], cac[:], ACT.Silu,
                        bias=cvb[:, db:db + 1]),
                        waits=[("v", vv)] + xcb_free[pj], incs=[("s", 1)])["s"]
                    xcb_free[pj] = []
                    state["cac_free"] = [("s", xcv)]
                    xc_vals.append(xcv)
                bw = bank_free[id(pX)]
                for kb in range(NDB):
                    p.op("tensor", lambda t, kb=kb, pj=pj: t.matmul(
                        pX[0:96, :], wx[:, kb * 96:(kb + 1) * 96],
                        xcb[pj][:, kb * TQ:(kb + 1) * TQ],
                        start=(kb == 0), stop=(kb == NDB - 1)),
                        waits=([("s", xc_vals[NDB - 1])] + bw) if kb == 0 else [],
                        incs=[("t", 1)] if kb == NDB - 1 else [])
                tv = p.now("t")
                ev = p.op("scalar", lambda s, pj=pj: s.copy(xdbp[pj][:], pX[0:96, :]),
                          waits=[("t", tv)] + xdbp_free[pj], incs=[("s", 1)])["s"]
                xdbp_free[pj] = []
                bank_free[id(pX)] = [("s", ev)]
                hw = p.op("sync", lambda s, pj=pj: s.dma_start(
                    out=ccin[pj].ap(), in_=xdbp[pj][:]),
                    waits=[("s", ev)] + ccin_free[pj], incs=[("hw", 16)])["hw"]
                ccin_free[pj] = []
                if K_CC:
                    cc = p.op("gpsimd", lambda g, pj=pj: g.collective_compute(
                        "AllReduce", ALU.add, replica_groups=GROUPS,
                        ins=[ccin[pj].ap().opt()], outs=[ccout[pj].ap().opt()]),
                        waits=[("hw", hw)] + state.get(("ccout_free", pj), []),
                        incs=[("cc", 1)])["cc"]
                else:
                    cc = p.op("gpsimd", lambda g, pj=pj: g.dma_start(
                        out=ccout[pj].ap(), in_=ccin[pj].ap()),
                        waits=[("hw", hw)] + state.get(("ccout_free", pj), []),
                        incs=[("cc", 16)])["cc"]
                gdt = p.op("gpsimd", lambda g, pj=pj: g.dma_start(
                    out=dtf[pj][:], in_=ccout[pj].ap()[0:RK, :]),
                    waits=[("cc", cc)] + dtf_free[pj], incs=[("g", 16)])["g"]
                dtf_free[pj] = []
                for buf, r0 in ((bca, 64), (cca, 80)):
                    sl = ccout[pj].ap()[r0:r0 + S, :]
                    src = bass.AP(sl.tensor, sl.offset, [[0, 128]] + list(sl.ap))
                    p.op("gpsimd", lambda g, buf=buf, pj=pj, src=src: g.dma_start(
                        out=buf[pj][:].rearrange("p (s t) -> p s t", s=S), in_=src),
                        waits=([("cc", cc)] + bca_free[pj]) if buf is bca else [("cc", cc)],
                        incs=[("g", 16)])
                bca_free[pj] = []
                state[("ccout_free", pj)] = [("g", p.now("g"))]
                state[("a", d, j)] = dict(gx=gx, gdt=gdt, gbc=p.now("g"), xc=xc_vals,
                                          pj=pj, t0=t0)

            def stage_b(d, j):
                info = state[("a", d, j)]
                pj = info["pj"]
                t0 = info["t0"]
                dl_vals = []
                for db in range(NDB):
                    bank = pD[db % 2]
                    bw = bank_free[id(bank)]
                    p.op("tensor", lambda t, db=db, pj=pj, bank=bank: t.matmul(
                        bank[:], wdt[d][:, db * 128:(db + 1) * 128], dtf[pj][:],
                        start=True, stop=True),
                        waits=[("g", info["gdt"])] + bw, incs=[("t", 1)])
                    tv = p.now("t")
                    p.op("scalar", lambda s, db=db, bank=bank: s.activation(
                        sp[0][:], bank[:], ACT.Abs, bias=dtb[d][:, db:db + 1]),
                        waits=[("t", tv)] + state.get("sp_free", []), incs=[("s", 1)])
                    p.op("scalar", lambda s, db=db, bank=bank: s.activation(
                        sp[1][:], bank[:], ACT.Relu, bias=dtb[d][:, db:db + 1]),
                        incs=[("s", 1)])
                    p.op("scalar", lambda s: s.activation(
                        sp[2][:], sp[0][:], ACT.Exp, scale=-1.0), incs=[("s", 1)])
                    sv = p.op("scalar", lambda s: s.activation(
                        sp[0][:], sp[2][:], ACT.Ln, bias=1.0), incs=[("s", 1)])["s"]
                    bank_free[id(bank)] = [("s", sv)]
                    dv = p.op("vector", lambda v, db=db: v.tensor_tensor(
                        dlt[:, db * TQ:(db + 1) * TQ], sp[0][:], sp[1][:], ALU.add),
                        waits=[("s", sv)] + state.get("dlt_free", []),
                        incs=[("v", 1)])["v"]
                    p.op("vector", lambda v, db=db, pj=pj: v.tensor_tensor(
                        dub[:, db * TQ:(db + 1) * TQ], dlt[:, db * TQ:(db + 1) * TQ],
                        xcb[pj][:, db * TQ:(db + 1) * TQ], ALU.mult),
                        waits=[("s", info["xc"][db])], incs=[("v", 1)])
                    dl_vals.append(dv)
                rev = (d == 1)
                for si in range(S):
                    slot_vals = []
                    for db in range(NDB):
                        slot = (si * NDB + db) % 4
                        sv = p.op("scalar", lambda s, db=db, si=si, slot=slot: s.activation(
                            dAp[slot][:], dlt[:, db * TQ:(db + 1) * TQ], ACT.Exp,
                            scale=asb[d][:, db * S + si:db * S + si + 1]),
                            waits=[("v", dl_vals[db])] + dA_free[slot], incs=[("s", 1)])["s"]
                        slot_vals.append((slot, sv))
                    bsl = bca[pj][:, si * TQ:(si + 1) * TQ]
                    p.op("vector", lambda v, pj=pj, bsl=bsl: v.tensor_tensor(
                        wbf[:], dub[:], _mid_bcast(bsl, NDB), ALU.mult),
                        waits=[("g", info["gbc"]), ("v", p.now("v"))], incs=[("v", 1)])
                    for db in range(NDB):
                        slot, sv = slot_vals[db]
                        if j == 0:
                            init = 0.0
                        else:
                            init = cry[:, db * S + si:db * S + si + 1]
                        d0 = dAp[slot][:]
                        d1 = wbf[:, db * TQ:(db + 1) * TQ]
                        o = hbf[:, db * TQ:(db + 1) * TQ]
                        if rev:
                            d0 = d0[:, TQ - 1::-1]
                            d1 = d1[:, TQ - 1::-1]
                            o = o[:, TQ - 1::-1]
                        vv = p.op("vector", lambda v, d0=d0, d1=d1, o=o, init=init: v.tensor_tensor_scan(
                            o, d0, d1, init, ALU.mult, ALU.add),
                            waits=[("s", sv)], incs=[("v", 1)])["v"]
                        dA_free[slot] = [("v", vv)]
                    last = 0 if rev else TQ - 1
                    for db in range(NDB):
                        p.op("vector", lambda v, db=db, si=si, last=last: v.tensor_copy(
                            cry[:, db * S + si:db * S + si + 1],
                            hbf[:, db * TQ + last:db * TQ + last + 1]), incs=[("v", 1)])
                    csl = cca[pj][:, si * TQ:(si + 1) * TQ]
                    if si == 0:
                        p.op("vector", lambda v, csl=csl: v.tensor_tensor(
                            yac[:], hbf[:], _mid_bcast(csl, NDB), ALU.mult),
                            incs=[("v", 1)])
                    else:
                        p.op("vector", lambda v, csl=csl: v.tensor_tensor(
                            pbf[:], hbf[:], _mid_bcast(csl, NDB), ALU.mult),
                            incs=[("v", 1)])
                        p.op("vector", lambda v: v.tensor_tensor(
                            yac[:], yac[:], pbf[:], ALU.add), incs=[("v", 1)])
                state["dlt_free"] = [("s", p.now("s"))]
                state["sp_free"] = [("v", p.now("v"))]
                for db in range(NDB):
                    p.op("vector", lambda v, db=db, pj=pj: v.scalar_tensor_tensor(
                        pbf[:, db * TQ:(db + 1) * TQ], xcb[pj][:, db * TQ:(db + 1) * TQ],
                        dsb[d][:, db:db + 1], yac[:, db * TQ:(db + 1) * TQ],
                        ALU.mult, ALU.add), incs=[("v", 1)])
                yv = p.op("vector", lambda v, pj=pj: v.tensor_tensor(
                    y2b[pj][:], pbf[:], szb[pj][:], ALU.mult),
                    waits=y2_free[pj], incs=[("v", 1)])["v"]
                y2_free[pj] = []
                ev_vals = []
                for fb in range(8):
                    bank = pO[fb % 2]
                    bw = bank_free[id(bank)]
                    for kb in range(NDB):
                        p.op("tensor", lambda t, fb=fb, kb=kb, pj=pj, bank=bank: t.matmul(
                            bank[:], wout[d][:, kb * DM + fb * 128:kb * DM + (fb + 1) * 128],
                            y2b[pj][:, kb * TQ:(kb + 1) * TQ],
                            start=(kb == 0), stop=(kb == NDB - 1)),
                            waits=([("v", yv)] + bw) if kb == 0 else [],
                            incs=[("t", 1)] if kb == NDB - 1 else [])
                    tv = p.now("t")
                    ev = p.op("scalar", lambda s, fb=fb, pj=pj, bank=bank: s.copy(
                        otp[pj][:, fb * TQ:(fb + 1) * TQ], bank[:]),
                        waits=[("t", tv)] + list(st_hw_otp[pj].items()), incs=[("s", 1)])["s"]
                    st_hw_otp[pj] = {}
                    bank_free[id(bank)] = [("s", ev)]
                    ev_vals.append(ev)
                qtr, t0r = t0 // TOK, t0 % TOK
                hw = p.op("sync", lambda s, pj=pj, qtr=qtr, t0r=t0r, d=d: s.dma_start(
                    out=accd[d].ap()[qtr, :, t0r:t0r + TQ].rearrange(
                        "(f q) t -> q f t", q=128),
                    in_=otp[pj][:].rearrange("q (f t) -> q f t", f=8)),
                    waits=[("s", ev_vals[-1])], incs=[("hw", 16)])["hw"]
                st_hw_otp[pj] = {"hw": hw}
                sx_free[pj] = [("t", p.now("t"))]
                y2_free[pj] = [("t", p.now("t"))]
                xib_free[pj] = [("v", p.now("v"))]
                xcb_free[pj] = [("v", p.now("v"))]
                szb_free[pj] = [("v", yv)]
                dtf_free[pj] = [("t", p.now("t"))]
                bca_free[pj] = [("v", p.now("v"))]
                xdbp_free[pj] = [("hw", hw)]
                ccin_free[pj] = [("cc", p.now("cc"))]

            for d in range(K_DIRS):
                state["w_in_free"] = [("t", p.now("t"))]
                state["cvw_free"] = [("v", p.now("v"))]
                state["cvb_free"] = [("s", p.now("s"))]
                state["wload"] = load_dir_weights(d)
                if PIPELINE:
                    stage_a(d, 0)
                    for j in range(K_NT):
                        if j + 1 < K_NT:
                            stage_a(d, j + 1)
                        stage_b(d, j)
                else:
                    for j in range(K_NT):
                        stage_a(d, j)
                        stage_b(d, j)

            p.wait("sync", [("hw", p.now("hw"))])
            p.wait("gpsimd", [("g", p.now("g")), ("cc", p.now("cc"))])
            mamba_hw = p.now("hw")
            for eng in ("sync", "scalar", "vector", "tensor", "gpsimd"):
                p.op(eng, lambda e: e.nop(), incs=[("ph", 1)])
            phv = p.now("ph")
            for eng in ("sync", "scalar", "vector", "tensor", "gpsimd"):
                p.wait(eng, [("ph", phv)])

        rs_vals = []
        for d in range(2):
            if K_CC:
                cc = p.op("gpsimd", lambda g, d=d: g.collective_compute(
                    "ReduceScatter", ALU.add, replica_groups=GROUPS,
                    ins=[accd[d].ap().opt()], outs=[rsd[d].ap().opt()]),
                    waits=[("hw", mamba_hw)] if d == 0 else [],
                    incs=[("cc", 1)])["cc"]
            else:
                cc = p.op("gpsimd", lambda g, d=d: g.dma_start(
                    out=rsd[d].ap(), in_=accd[d].ap()[0]),
                    waits=[("hw", mamba_hw)] if d == 0 else [],
                    incs=[("cc", 16)])["cc"]
            rs_vals.append(cc)

        if not K_FFN:
            p.op("gpsimd", lambda g: g.dma_start(
                out=out.ap(), in_=rsd[0].ap()), waits=[("cc", p.now("cc"))],
                incs=[("g", 16)])
            p.wait("gpsimd", [("g", p.now("g"))])

            @block.sync
            def _(sync):
                for f in p.streams["sync"]:
                    f(sync)

            @block.scalar
            def _(scalar):
                for f in p.streams["scalar"]:
                    f(scalar)

            @block.vector
            def _(vector):
                for f in p.streams["vector"]:
                    f(vector)

            @block.tensor
            def _(tensor):
                for f in p.streams["tensor"]:
                    f(tensor)

            @block.gpsimd
            def _(gpsimd):
                for f in p.streams["gpsimd"]:
                    f(gpsimd)
            return nc
        with ExitStack() as fctx:
          try:
            fsb = lambda n, sh, dt: fctx.enter_context(nc.sbuf_tensor(n, sh, dt))
            xnb = fsb("xnb", [128, 8 * TOK], BF16)
            ytot = fsb("ytot", [128, 8 * TOK], BF16)
            sqf = fsb("sqf", [128, TQ], F32)
            sqg = fsb("sqg", [128, TQ], F32)
            rows = fsb("rows", [128, 4 * TQ], F32)
            ids = fsb("ids", [128, 128], F32)
            on1 = fsb("on1", [128, 1], F32)
            onr = fsb("onr", [128, 128], F32)
            rou = fsb("rou", [128, 8 * E], BF16)
            lgs = fsb("lgs", [128, TQ], F32)
            mxc = fsb("mxc", [128, 1], F32)
            oh8 = fsb("oh8", [128, 8], F32)
            lgt = fsb("lgt", [128, 8], F32)
            mxr = fsb("mxr", [128, TQ], F32)
            mx8 = fsb("mx8", [128, TQ], F32)
            mskT = fsb("mskT", [128, NTT * TQ], BF16)

            p.op("gpsimd", lambda g: g.dma_start(out=ids[:], in_=ident.ap()),
                 incs=[("g", 16)])
            p.op("gpsimd", lambda g: g.dma_start(out=on1[:], in_=ones1.ap()),
                 incs=[("g", 16)])
            p.op("gpsimd", lambda g: g.dma_start(out=onr[0:1, :], in_=onesr.ap()),
                 incs=[("g", 16)])
            p.op("gpsimd", lambda g: g.dma_start(
                out=rou[:].rearrange("p (k e) -> p k e", k=8),
                in_=routT.ap().rearrange("(k p) e -> p k e", p=128)),
                incs=[("g", 16)])
            gconst = p.now("g")

            def layernorm(src_getter, dst_write, base_waits):
                bw1 = bank_free[id(pD[0])]
                bw2 = bank_free[id(pD[1])]
                sq_free = state.get("sq_free", [[], []])
                for fb in range(8):
                    src, w8 = src_getter(fb)
                    slot = (sqf, sqg)[fb % 2]
                    p.op("tensor", lambda t, src=src, fb=fb: t.matmul(
                        pD[0][0:1, :], on1[:], src, start=(fb == 0), stop=(fb == 7)),
                        waits=(base_waits + w8 + bw1 + [("g", gconst)]) if fb == 0 else w8,
                        incs=[("t", 1)])
                    sv = p.op("scalar", lambda s, src=src, slot=slot: s.square(
                        slot[:], src), waits=w8 + sq_free[fb % 2], incs=[("s", 1)])["s"]
                    tv2 = p.op("tensor", lambda t, fb=fb, slot=slot: t.matmul(
                        pD[1][0:1, :], on1[:], slot[:], start=(fb == 0), stop=(fb == 7)),
                        waits=[("s", sv)] + (bw2 if fb == 0 else []),
                        incs=[("t", 1)])["t"]
                    sq_free[fb % 2] = [("t", tv2)]
                state["sq_free"] = sq_free
                tv = p.now("t")
                r = lambda i: rows[0:1, i * TQ:(i + 1) * TQ]
                p.op("vector", lambda v: v.tensor_scalar_mul(
                    r(0), pD[0][0:1, :], 1.0 / DM), waits=[("t", tv)], incs=[("v", 1)])
                p.op("vector", lambda v: v.tensor_scalar_mul(
                    r(1), pD[1][0:1, :], 1.0 / DM), incs=[("v", 1)])
                p.op("vector", lambda v: v.tensor_tensor(
                    r(2), r(0), r(0), ALU.mult), incs=[("v", 1)])
                p.op("vector", lambda v: v.tensor_tensor(
                    r(1), r(1), r(2), ALU.subtract), incs=[("v", 1)])
                p.op("vector", lambda v: v.tensor_scalar(
                    r(1), r(1), EPS, None, ALU.add), incs=[("v", 1)])
                p.op("vector", lambda v: v.reciprocal(r(2), r(1)), incs=[("v", 1)])
                vv = p.now("v")
                sv = p.op("scalar", lambda s: s.activation(
                    r(3), r(2), ACT.Sqrt), waits=[("v", vv)], incs=[("s", 1)])["s"]
                vv = p.op("vector", lambda v: v.scalar_tensor_tensor(
                    r(2), r(0), -1.0, r(3), ALU.mult, ALU.mult),
                    waits=[("s", sv)], incs=[("v", 1)])["v"]
                bv1 = bank_free[id(pO[0])]
                bv2 = bank_free[id(pO[1])]
                p.op("tensor", lambda t: t.matmul(
                    pO[0][:], onr[0:1, :], rows[0:1, 3 * TQ:4 * TQ], start=True,
                    stop=True), waits=[("v", vv)] + bv1, incs=[("t", 1)])
                p.op("tensor", lambda t: t.matmul(
                    pO[1][:], onr[0:1, :], rows[0:1, 2 * TQ:3 * TQ], start=True,
                    stop=True), waits=bv2, incs=[("t", 1)])
                tv = p.now("t")
                last_v = None
                for fb in range(8):
                    src, w8 = src_getter(fb)
                    p.op("vector", lambda v, src=src: v.tensor_tensor(
                        sqf[:], src, pO[0][:], ALU.mult),
                        waits=([("t", tv)] if fb == 0 else []) + w8, incs=[("v", 1)])
                    last_v = dst_write(fb)
                state["sq_free"] = [[("v", p.now("v"))], state["sq_free"][1]]
                bank_free[id(pD[0])] = [("v", p.now("v"))]
                bank_free[id(pD[1])] = [("v", p.now("v"))]
                bank_free[id(pO[0])] = [("v", p.now("v"))]
                bank_free[id(pO[1])] = [("v", p.now("v"))]
                return last_v

            xn_vals = [0] * NTT
            msk_vals = [0] * NTT
            pctx = ExitStack()
            u32 = pctx.enter_context(nc.sbuf_tensor("u32", [128, 8 * TQ], F32))
            rstg = pctx.enter_context(nc.sbuf_tensor("rstg", [128, 8 * TQ], BF16))
            for nt in range(NTT):
                c0 = nt * TQ
                hw3 = p.op("sync", lambda s, c0=c0: s.dma_start(
                    out=u32[:].rearrange("p (f t) -> p f t", f=8),
                    in_=xtok.ap()[:, c0:c0 + TQ].rearrange("(f q) t -> q f t", q=128)),
                    waits=state.get("u32_free", []), incs=[("hw", 16)])["hw"]
                hw1 = p.op("sync", lambda s, c0=c0: s.dma_start(
                    out=rstg[:].rearrange("p (f t) -> p f t", f=8),
                    in_=rsd[0].ap()[:, c0:c0 + TQ].rearrange("(f q) t -> q f t", q=128)),
                    waits=[("cc", rs_vals[0])] + state.get("rstg_free", []),
                    incs=[("hw", 16)])["hw"]
                v1 = p.op("vector", lambda v: v.tensor_tensor(
                    u32[:], u32[:], rstg[:], ALU.add),
                    waits=[("hw", hw3), ("hw", hw1)], incs=[("v", 1)])["v"]
                hw2 = p.op("sync", lambda s, c0=c0: s.dma_start(
                    out=rstg[:].rearrange("p (f t) -> p f t", f=8),
                    in_=rsd[1].ap()[:, c0:c0 + TQ].rearrange("(f q) t -> q f t", q=128)),
                    waits=[("cc", rs_vals[1]), ("v", v1)], incs=[("hw", 16)])["hw"]
                xv = p.op("vector", lambda v: v.tensor_tensor(
                    u32[:], u32[:], rstg[:], ALU.add),
                    waits=[("hw", hw2)], incs=[("v", 1)])["v"]
                state["rstg_free"] = [("v", xv)]

                def src_getter(fb, xv=xv):
                    return u32[:, fb * TQ:(fb + 1) * TQ], [("v", xv)]

                def dst_write(fb, nt=nt):
                    return p.op("vector", lambda v, fb=fb, nt=nt: v.tensor_tensor(
                        xnb[:, fb * TOK + nt * TQ: fb * TOK + nt * TQ + TQ],
                        sqf[:], pO[1][:], ALU.add), incs=[("v", 1)])["v"]

                if K_LN:
                    xn_vals[nt] = layernorm(src_getter, dst_write, [])
                else:
                    for fb in range(8):
                        xn_vals[nt] = p.op("vector", lambda v, fb=fb, nt=nt: v.tensor_copy(
                            xnb[:, fb * TOK + nt * TQ: fb * TOK + nt * TQ + TQ],
                            u32[:, fb * TQ:(fb + 1) * TQ]),
                            waits=[("v", xv)], incs=[("v", 1)])["v"]
                state["u32_free"] = [("v", xn_vals[nt])]
                if not K_RT:
                    mv = p.op("vector", lambda v, nt=nt: v.memset(
                        mskT[0:E, nt * TQ:(nt + 1) * TQ], 1.0), incs=[("v", 1)])["v"]
                    msk_vals[nt] = p.op("sync", lambda s, nt=nt: s.dma_start(
                        out=mskd.ap()[:, nt * TQ:(nt + 1) * TQ],
                        in_=mskT[0:E, nt * TQ:(nt + 1) * TQ]),
                        waits=[("v", mv)], incs=[("hw", 16)])["hw"]
                    continue
                bw = bank_free[id(pX)]
                for kb in range(8):
                    p.op("tensor", lambda t, kb=kb, nt=nt: t.matmul(
                        pX[0:E, :], rou[:, kb * E:(kb + 1) * E],
                        xnb[:, kb * TOK + nt * TQ: kb * TOK + nt * TQ + TQ],
                        start=(kb == 0), stop=(kb == 7)),
                        waits=([("v", xn_vals[nt])] + bw) if kb == 0 else [],
                        incs=[("t", 1)] if kb == 7 else [])
                tv = p.now("t")
                lv = p.op("vector", lambda v: v.tensor_copy(
                    lgs[0:E, :], pX[0:E, :]), waits=[("t", tv)], incs=[("v", 1)])["v"]
                bank_free[id(pX)] = [("v", lv)]
                mv = 0
                for tt in range(4):
                    bw = bank_free[id(pD[0])]
                    p.op("tensor", lambda t, tt=tt: t.transpose(
                        pD[0][:, 0:E], lgs[0:E, tt * 128:(tt + 1) * 128], ids[0:E, 0:E]),
                        waits=[("v", lv), ("g", gconst)] + bw, incs=[("t", 1)])
                    tv = p.now("t")
                    sv8 = p.op("scalar", lambda s: s.copy(
                        lgt[:], pD[0][:, 0:E]), waits=[("t", tv)], incs=[("s", 1)])["s"]
                    bank_free[id(pD[0])] = [("s", sv8)]
                    mxv = p.op("vector", lambda v: v.reduce_max(
                        mxc[:], lgt[:], axis=AXX), waits=[("s", sv8)],
                        incs=[("v", 1)])["v"]
                    bw = bank_free[id(pD[1])]
                    p.op("tensor", lambda t: t.transpose(
                        pD[1][0:1, 0:128], mxc[:, 0:1], ids[:]),
                        waits=[("v", mxv)] + bw, incs=[("t", 1)])
                    tv = p.now("t")
                    sr = p.op("scalar", lambda s, tt=tt: s.copy(
                        mxr[0:1, tt * 128:(tt + 1) * 128], pD[1][0:1, 0:128]),
                        waits=[("t", tv)], incs=[("s", 1)])["s"]
                    bank_free[id(pD[1])] = [("s", sr)]
                hwm = p.op("sync", lambda s, nt=nt: s.dma_start(
                    out=mxd.ap()[:, nt * TQ:(nt + 1) * TQ], in_=mxr[0:1, :]),
                    waits=[("s", p.now("s"))], incs=[("hw", 16)])["hw"]
                msl = mxd.ap()[0:1, nt * TQ:(nt + 1) * TQ]
                gvm = p.op("gpsimd", lambda g, msl=msl: g.dma_start(
                    out=mx8[0:E, :], in_=_bap(msl, E)),
                    waits=[("hw", hwm), ("v", p.now("v"))], incs=[("g", 16)])["g"]
                mv = p.op("vector", lambda v, nt=nt: v.tensor_tensor(
                    mskT[0:E, nt * TQ:(nt + 1) * TQ], lgs[0:E, :], mx8[0:E, :],
                    ALU.is_ge), waits=[("g", gvm)], incs=[("v", 1)])["v"]
                msk_vals[nt] = mv
                msk_vals[nt] = p.op("sync", lambda s, nt=nt: s.dma_start(
                    out=mskd.ap()[:, nt * TQ:(nt + 1) * TQ],
                    in_=mskT[0:E, nt * TQ:(nt + 1) * TQ]),
                    waits=[("v", mv)], incs=[("hw", 16)])["hw"]

            pctx.close()
            pre_free = [("v", p.now("v")), ("t", p.now("t")), ("s", p.now("s"))]
            ectx = ExitStack()
            esb = lambda n, sh, dt: ectx.enter_context(nc.sbuf_tensor(n, sh, dt))
            ups = esb("ups", [128, 8 * HF], BF16)
            gts = esb("gts", [128, 8 * HF], BF16)
            dns = esb("dns", [128, 16 * DM], BF16)
            sgs = esb("sgs", [128, TQ], BF16)
            hac = esb("hac", [128, 16 * TQ], BF16)
            t1b = esb("t1b", [128, TQ], BF16)
            mbc = esb("mbc", [128, TQ], BF16)
            if K_FP < 2:
                p.op("sync", lambda s: s.dma_start(
                    out=out.ap()[:, 0:TQ].rearrange("(f q) t -> q f t", q=128),
                    in_=u32[:].rearrange("q (f t) -> q f t", f=8)),
                    waits=[("v", p.now("v"))], incs=[("hw", 16)])
                p.wait("sync", [("hw", p.now("hw"))])
                ectx.close()
                raise _EarlyEmit()
            def load_w(dst, src_ap, kblk, waits):
                vals = []
                for ck in range(4):
                    kw = kblk // 4
                    g = p.op("gpsimd", lambda gg, ck=ck, kw=kw: gg.dma_start(
                        out=dst[:].rearrange("p (k c) -> p k c", k=kblk)[:, ck * kw:(ck + 1) * kw, :],
                        in_=src_ap.rearrange("(k p) c -> p k c", p=128)[:, ck * kw:(ck + 1) * kw, :]),
                        waits=waits if ck == 0 else [], incs=[("g", 16)])["g"]
                    vals.append(g)
                return vals

            shu = load_w(ups, wupsT.ap(), 8, pre_free)
            shg = load_w(gts, wgsT.ap(), 8, pre_free)
            shd = load_w(dns, wdsT.ap(), 16, pre_free)

            def expert_pass(masked, e, wu, wg, wd):
                for nt in range(NTT):
                    if masked:
                        msl = mskd.ap()[e:e + 1, nt * TQ:(nt + 1) * TQ]
                        state["mbc_g"] = p.op("gpsimd", lambda g, msl=msl: g.dma_start(
                            out=mbc[:], in_=_bap(msl, 128)),
                            waits=[("hw", msk_vals[nt])] + state.get("mbc_free", []),
                            incs=[("g", 16)])["g"]
                    for hb in range(16):
                        ck = hb // 4
                        bw0 = bank_free[id(pA[0])]
                        bw1 = bank_free[id(pA[1])]
                        for kb in range(8):
                            p.op("tensor", lambda t, hb=hb, kb=kb, nt=nt: t.matmul(
                                pA[0][:], ups[:, kb * HF + hb * 128:kb * HF + (hb + 1) * 128],
                                xnb[:, kb * TOK + nt * TQ:kb * TOK + nt * TQ + TQ],
                                start=(kb == 0), stop=(kb == 7)),
                                waits=([("g", wu[-1]), ("v", xn_vals[nt])] + bw0) if kb == 0 else [],
                                incs=[("t", 1)] if kb == 7 else [])
                        for kb in range(8):
                            p.op("tensor", lambda t, hb=hb, kb=kb, nt=nt: t.matmul(
                                pA[1][:], gts[:, kb * HF + hb * 128:kb * HF + (hb + 1) * 128],
                                xnb[:, kb * TOK + nt * TQ:kb * TOK + nt * TQ + TQ],
                                start=(kb == 0), stop=(kb == 7)),
                                waits=([("g", wg[-1])] + bw1) if kb == 0 else [],
                                incs=[("t", 1)] if kb == 7 else [])
                        tv = p.now("t")
                        sv = p.op("scalar", lambda s: s.activation(
                            sgs[:], pA[1][:], ACT.Silu), waits=[("t", tv)],
                            incs=[("s", 1)])["s"]
                        bank_free[id(pA[1])] = [("s", sv)]
                        hv = p.op("vector", lambda v, hb=hb: v.tensor_tensor(
                            hac[:, hb * TQ:(hb + 1) * TQ], pA[0][:], sgs[:], ALU.mult),
                            waits=[("s", sv)], incs=[("v", 1)])["v"]
                        bank_free[id(pA[0])] = [("v", hv)]
                    hv = p.now("v")
                    state["t_after_ug"] = p.now("t")
                    for cb in range(8):
                        bw2 = bank_free[id(pA[2])]
                        for kb in range(16):
                            p.op("tensor", lambda t, cb=cb, kb=kb: t.matmul(
                                pA[2][:], dns[:, kb * DM + cb * 128:kb * DM + (cb + 1) * 128],
                                hac[:, kb * TQ:(kb + 1) * TQ],
                                start=(kb == 0), stop=(kb == 15)),
                                waits=([("v", hv), ("g", wd[-1])] + bw2) if kb == 0 else [],
                                incs=[("t", 1)] if kb == 15 else [])
                        tv = p.now("t")
                        dst = (slice(None), slice(cb * TOK + nt * TQ, cb * TOK + nt * TQ + TQ))
                        if masked:
                            p.op("vector", lambda v: v.tensor_tensor(
                                t1b[:], pA[2][:], mbc[:], ALU.mult),
                                waits=[("t", tv), ("g", state["mbc_g"])], incs=[("v", 1)])
                            vv = p.op("vector", lambda v, dst=dst: v.tensor_tensor(
                                ytot[dst], ytot[dst], t1b[:], ALU.add), incs=[("v", 1)])["v"]
                            state["mbc_free"] = [("v", vv)]
                        else:
                            vv = p.op("vector", lambda v, dst=dst: v.tensor_copy(
                                ytot[dst], pA[2][:]), waits=[("t", tv)], incs=[("v", 1)])["v"]
                        bank_free[id(pA[2])] = [("v", vv)]
                    if nt == NTT - 1:
                        state["ups_free"] = [("t", state["t_after_ug"])]
                        state["gts_free"] = [("t", state["t_after_ug"])]
                        state["dns_free"] = [("t", p.now("t"))]

            expert_pass(False, -1, shu, shg, shd)
            for e in range(K_E):
                wu = load_w(ups, upT.ap()[e], 8, state.get("ups_free", []))
                wg = load_w(gts, gateT.ap()[e], 8, state.get("gts_free", []))
                wd = load_w(dns, downT.ap()[e], 16, state.get("dns_free", []))
                expert_pass(True, e, wu, wg, wd)

            if K_FP < 3:
                p.wait("sync", [("hw", p.now("hw"))])
                p.wait("vector", [("v", p.now("v"))])
                ectx.close()
                raise _EarlyEmit()
            ectx.close()
            exp_free = [("v", p.now("v")), ("t", p.now("t")), ("s", p.now("s")),
                        ("g", p.now("g"))]
            u32 = fctx.enter_context(nc.sbuf_tensor("u32b", [128, 8 * TQ], F32))
            state["u32_free"] = exp_free
            st_val = 0
            for nt in range(NTT):
                xv = 0
                for fb in range(8):
                    sl = slice(fb * TOK + nt * TQ, fb * TOK + nt * TQ + TQ)
                    xv = p.op("vector", lambda v, fb=fb, sl=sl: v.tensor_tensor(
                        u32[:, fb * TQ:(fb + 1) * TQ], xnb[:, sl], ytot[:, sl], ALU.add),
                        waits=(([("hw", st_val)] if st_val else [])
                               + state.get("u32_free", [])) if fb == 0 else [],
                        incs=[("v", 1)])["v"]
                state["u32_free"] = []

                def src_getter(fb, xv=xv):
                    return u32[:, fb * TQ:(fb + 1) * TQ], [("v", xv)]

                def dst_write(fb):
                    return p.op("vector", lambda v, fb=fb: v.scalar_tensor_tensor(
                        u32[:, fb * TQ:(fb + 1) * TQ], sqf[:], 1.0, pO[1][:],
                        ALU.mult, ALU.add), incs=[("v", 1)])["v"]

                lv = layernorm(src_getter, dst_write, [])
                st_val = p.op("sync", lambda s, nt=nt: s.dma_start(
                    out=out.ap()[:, nt * TQ:(nt + 1) * TQ].rearrange(
                        "(f q) t -> q f t", q=128),
                    in_=u32[:].rearrange("q (f t) -> q f t", f=8)),
                    waits=[("v", lv)], incs=[("hw", 16)])["hw"]
            if K_DBG == "mask":
                p.op("gpsimd", lambda g: g.dma_start(
                    out=out.ap()[0:8, :], in_=mskd.ap()),
                    waits=[("hw", st_val)], incs=[("g", 16)])
                p.wait("gpsimd", [("g", p.now("g"))])
            if K_DBG == "oh":
                with nc.allow_non_contiguous_dma(reason="debug"):
                    p.op("sync", lambda s: s.dma_start(
                        out=out.ap()[16:24, 0:128].rearrange("e t -> t e"),
                        in_=oh8[:], allow_slow_non_contiguous=True),
                        waits=[("hw", st_val)], incs=[("hw", 16)])
                    p.op("sync", lambda s: s.dma_start(
                        out=out.ap()[24:25, 0:128].rearrange("e t -> t e"),
                        in_=mxc[:], allow_slow_non_contiguous=True),
                        incs=[("hw", 16)])
                with nc.allow_non_contiguous_dma(reason="debug"):
                    p.op("sync", lambda s: s.dma_start(
                        out=out.ap()[34:42, 0:128].rearrange("e t -> t e"),
                        in_=lgt[:], allow_slow_non_contiguous=True),
                        incs=[("hw", 16)])
                p.op("gpsimd", lambda g: g.dma_start(
                    out=out.ap()[25:33, 0:2048], in_=mskT[0:8, :]),
                    waits=[("hw", p.now("hw"))], incs=[("g", 16)])
                p.wait("gpsimd", [("g", p.now("g"))])
                p.wait("sync", [("hw", p.now("hw"))])
            if K_DBG == "logits":
                p.op("sync", lambda s: s.dma_start(
                    out=out.ap()[8:16, 0:TQ], in_=lgs[0:8, :]),
                    waits=[("hw", st_val)], incs=[("hw", 16)])
                p.wait("sync", [("hw", p.now("hw"))])
            p.wait("sync", [("hw", st_val)])
          except _EarlyEmit:
            pass
          if True:
            @block.sync
            def _(sync):
                for f in p.streams["sync"]:
                    f(sync)

            @block.scalar
            def _(scalar):
                for f in p.streams["scalar"]:
                    f(scalar)

            @block.vector
            def _(vector):
                for f in p.streams["vector"]:
                    f(vector)

            @block.tensor
            def _(tensor):
                for f in p.streams["tensor"]:
                    f(tensor)

            @block.gpsimd
            def _(gpsimd):
                for f in p.streams["gpsimd"]:
                    f(gpsimd)

    return nc


def _prep_core_inputs(c, full):
    b, q = c // 4, c % 4
    f = lambda k: np.ascontiguousarray(full[k])
    qs = slice(q * DQ, (q + 1) * DQ)
    ins = {}
    xb = np.ascontiguousarray(full["L0_attn_x"][b].T)
    ins["x"] = xb
    ins["xtok"] = np.ascontiguousarray(xb[:, q * TOK:(q + 1) * TOK])
    winT, convw, convb, dtbi, alogs, dcols, wxTs, wdtTs, woutTs = \
        [], [], [], [], [], [], [], [], []
    for tag in ("f", "b"):
        w_in = full[f"L0_attn_{tag}_in_proj_w"]
        wq = np.concatenate([w_in[qs], w_in[DI + q * DQ: DI + (q + 1) * DQ]], 0)
        winT.append(wq.T)
        cw = full[f"L0_attn_{tag}_conv_w"][qs]
        if tag == "b":
            cw = cw[:, ::-1]
        convw.append(np.ascontiguousarray(cw.reshape(NDB, 128, CV).transpose(1, 0, 2)
                                          ).reshape(128, NDB * CV))
        convb.append(full[f"L0_attn_{tag}_conv_b"][qs].reshape(NDB, 128).T)
        dtbi.append(full[f"L0_attn_{tag}_dt_proj_b"][qs].reshape(NDB, 128).T)
        alogs.append(np.ascontiguousarray(
            full[f"L0_attn_{tag}_A_log"][qs].reshape(NDB, 128, S).transpose(1, 0, 2)
        ).reshape(128, NDB * S))
        dcols.append(full[f"L0_attn_{tag}_D"][qs].reshape(NDB, 128).T)
        wxTs.append(full[f"L0_attn_{tag}_x_proj_w"][:, qs].T)
        wdtTs.append(full[f"L0_attn_{tag}_dt_proj_w"][qs].T)
        woutTs.append(full[f"L0_attn_{tag}_out_proj_w"][:, qs].T)
    st = lambda arrs: np.ascontiguousarray(np.stack(arrs).astype(np.float32))
    ins["winT"] = st(winT)
    ins["convw"] = st(convw)
    ins["convb"] = st(convb)
    ins["dtbi"] = st(dtbi)
    ins["alog"] = st(alogs)
    ins["dcol"] = st(dcols)
    ins["wxT"] = st(wxTs)
    ins["wdtT"] = st(wdtTs)
    ins["woutT"] = st(woutTs)
    ins["upT"] = np.ascontiguousarray(full["L0_ffn_up"].transpose(0, 2, 1))
    ins["gateT"] = np.ascontiguousarray(full["L0_ffn_gate"].transpose(0, 2, 1))
    ins["downT"] = np.ascontiguousarray(full["L0_ffn_down"].transpose(0, 2, 1))
    ins["wupsT"] = np.ascontiguousarray(full["L0_ffn_w_up_s"].T)
    ins["wgsT"] = np.ascontiguousarray(full["L0_ffn_w_gate_s"].T)
    ins["wdsT"] = np.ascontiguousarray(full["L0_ffn_w_down_s"].T)
    ins["routT"] = np.ascontiguousarray(full["L0_ffn_router"].T)
    ins["ident"] = np.eye(128, dtype=np.float32)
    ins["ones1"] = np.ones((128, 1), dtype=np.float32)
    ins["onesr"] = np.ones((1, 128), dtype=np.float32)
    return ins


def kernel(**inputs):
    nc = build_program()
    in_maps = [_prep_core_inputs(c, inputs) for c in range(8)]
    res = run_bass_kernel_spmd(nc, in_maps, core_ids=list(range(8)))
    outs = [np.asarray(r["out"], dtype=np.float32).T for r in res.results]
    b0 = np.concatenate(outs[0:4], axis=0)
    b1 = np.concatenate(outs[4:8], axis=0)
    return np.stack([b0, b1], axis=0)
